# version 35
# baseline (speedup 1.0000x reference)
import sys
for _p in ("/opt/trn_rl_repo",):
    if _p not in sys.path:
        sys.path.append(_p)
from contextlib import ExitStack
import numpy as np
import ml_dtypes
import concourse.bass as bass
import concourse.mybir as mybir
from concourse.bass_utils import run_bass_kernel_spmd

F32 = mybir.dt.float32
BF16 = mybir.dt.bfloat16
AF = mybir.ActivationFunctionType
ALU = mybir.AluOpType
AX = mybir.AxisListType
BF = ml_dtypes.bfloat16

C = 256
V = 1024
EPS = 1e-5

def host_consts():
    p = np.arange(128)
    i256 = np.arange(256)
    selL1 = np.zeros((32, 512), np.float32)
    P = np.arange(512)
    selL1[8 * (P // 128) + (P % 128) // 16, P] = 1.0
    selR1 = np.zeros((32, 512), np.float32)
    f = np.arange(512)
    selR1[f // 16, f] = 1.0
    selL2 = np.zeros((16, 128), np.float32)
    selL2[p % 16, p] = 1.0
    selR2 = np.zeros((16, 512), np.float32)
    selR2[f % 16, f] = 1.0
    dmaxT = np.zeros((128, 512), np.float32)
    causT = np.zeros((128, 512), np.float32)
    for jb in range(2):
        j = jb * 128 + p[:, None]
        dmaxT[:, jb * 256:(jb + 1) * 256] = np.maximum(j - i256[None, :] - 1, 0)
        causT[:, jb * 256:(jb + 1) * 256] = (j > i256[None, :])
    expc = np.zeros((128, 8), np.float32)
    expc[:, 0] = p
    expc[:, 1] = 128 + p
    expc[:, 2] = 255 - p
    expc[:, 3] = 127 - p
    expc[:, 4] = 256.0
    ident = np.eye(128, dtype=BF)
    ones1 = np.ones((1, 128), dtype=BF)
    return dict(selL1=selL1, selR1=selR1, selL2=selL2, selR2=selR2,
                dmaxT=dmaxT, causT=causT, expc=expc, ident=ident, ones1=ones1)


def build(T):
    NCH = T // C
    nc = bass.Bass("TRN2", target_bir_lowering=False)

    x_d = nc.dram_tensor("x", [T, V], F32, kind="ExternalInput")
    y_d = nc.dram_tensor("y", [T, V], F32, kind="ExternalOutput")
    dl_d = nc.dram_tensor("dl", [128, 1], F32, kind="ExternalInput")
    osc_d = nc.dram_tensor("osc", [128, 1], F32, kind="ExternalInput")
    qs_d = nc.dram_tensor("qs", [128, 8], F32, kind="ExternalInput")
    ks_d = nc.dram_tensor("ks", [128, 8], F32, kind="ExternalInput")
    g1r_d = nc.dram_tensor("g1r", [32, 128], F32, kind="ExternalInput")
    g1i_d = nc.dram_tensor("g1i", [32, 128], F32, kind="ExternalInput")
    g2r_d = nc.dram_tensor("g2r", [4, 256], F32, kind="ExternalInput")
    g2i_d = nc.dram_tensor("g2i", [4, 256], F32, kind="ExternalInput")
    br_d = nc.dram_tensor("br", [1, 512], F32, kind="ExternalInput")
    bi_d = nc.dram_tensor("bi", [1, 512], F32, kind="ExternalInput")
    selL1_d = nc.dram_tensor("selL1", [32, 512], F32, kind="ExternalInput")
    selR1_d = nc.dram_tensor("selR1", [32, 512], F32, kind="ExternalInput")
    selL2_d = nc.dram_tensor("selL2", [16, 128], F32, kind="ExternalInput")
    selR2_d = nc.dram_tensor("selR2", [16, 512], F32, kind="ExternalInput")
    dmaxT_d = nc.dram_tensor("dmaxT", [128, 512], F32, kind="ExternalInput")
    causT_d = nc.dram_tensor("causT", [128, 512], F32, kind="ExternalInput")
    expc_d = nc.dram_tensor("expc", [128, 8], F32, kind="ExternalInput")
    ident_d = nc.dram_tensor("ident", [128, 128], BF16, kind="ExternalInput")
    ones1_d = nc.dram_tensor("ones1", [1, 128], BF16, kind="ExternalInput")

    ctx = ExitStack()
    with ctx:
        sb = lambda name, shape, dt=F32: ctx.enter_context(nc.sbuf_tensor(name, shape, dt))
        ps = lambda name, shape, dt=F32: ctx.enter_context(nc.psum_tensor(name, shape, dt))
        xb = [sb("xb0", [128, 2048]), sb("xb1", [128, 2048]), sb("xb2", [128, 2048])]
        Xb = [sb("Xb0", [128, 2048], BF16), sb("Xb1", [128, 2048], BF16)]
        XTb = [sb("XT0", [128, 2048], BF16), sb("XT1", [128, 2048], BF16)]
        QKTb = [sb("QKT0", [128, 2048], BF16), sb("QKT1", [128, 2048], BF16)]
        PKb = [sb("PK0", [128, 2048], BF16), sb("PK1", [128, 2048], BF16)]
        mkT = [sb("mkT0", [128, 512], BF16), sb("mkT1", [128, 512], BF16)]
        intra_s = sb("intra_s", [128, 2048])
        s_sb = sb("s_sb", [128, 2048])
        gb = sb("gb", [128, 2048])
        sgb = sb("sgb", [128, 2048])
        ob = [sb("ob0", [128, 2048]), sb("ob1", [128, 2048])]
        Sst = sb("Sst", [128, 8192])
        Sbf = sb("Sbf", [128, 8192], BF16)
        WTr = sb("WTr", [128, 2048], BF16)
        WTi = sb("WTi", [128, 2048], BF16)
        WTiN = sb("WTiN", [128, 2048], BF16)
        MT = sb("MT", [128, 512])
        biasbf = sb("biasbf", [1, 1024], BF16)
        dl_s = sb("dl_s", [128, 1]); osc_s = sb("osc_s", [128, 1])
        qs_s = sb("qs_s", [128, 8]); ks_s = sb("ks_s", [128, 8]); qks = sb("qks", [128, 8])
        g1r_s = sb("g1r_s", [32, 128]); g1i_s = sb("g1i_s", [32, 128])
        g2r_s = sb("g2r_s", [16, 64]); g2i_s = sb("g2i_s", [16, 64])
        br_s = sb("br_s", [1, 512]); bi_s = sb("bi_s", [1, 512])
        selL1_s = sb("selL1_s", [32, 512]); selR1_s = sb("selR1_s", [32, 512])
        selL2_s = sb("selL2_s", [16, 128]); selR2_s = sb("selR2_s", [16, 512])
        dmaxT_s = sb("dmaxT_s", [128, 512]); causT_s = sb("causT_s", [128, 512])
        expc_s = sb("expc_s", [128, 8])
        ident_s = sb("ident_s", [128, 128], BF16)
        ones1_s = sb("ones1_s", [1, 128], BF16)
        sgd = sb("sgd", [128, 1]); lnd = sb("lnd", [128, 1])
        pj = sb("pj", [128, 2]); dvec = sb("dvec", [128, 2]); dCv = sb("dCv", [128, 1])
        ssq = sb("ssq", [128, 2]); ssum = sb("ssum", [128, 2])
        mu = sb("mu", [128, 2]); nt1 = sb("nt1", [128, 2]); vw = sb("vw", [128, 2])
        yy = sb("yy", [128, 2]); nt4 = sb("nt4", [128, 2])
        minv = sb("minv", [128, 2]); mnb = sb("mnb", [128, 2])
        junkbf = sb("junkbf", [128, 1024], BF16)
        Ps_s = sb("Ps_s", [32, 512]); P2_s = sb("P2_s", [16, 512]); T2_s = sb("T2_s", [128, 512])
        Wacc = ob

        trps = [ps("trps0", [128, 1024], BF16),
                ps("trps1", [128, 1024], BF16)]
        fps = [ps("fps0", [128, 512]), ps("fps1", [128, 512])]
        cps = [ps("cps0", [128, 512]), ps("cps1", [128, 512])]
        eps = [ps("eps0", [128, 512]), ps("eps1", [128, 512])]

        semD = ctx.enter_context(nc.semaphore("semD"))
        semX = ctx.enter_context(nc.semaphore("semX"))
        semO = ctx.enter_context(nc.semaphore("semO"))
        semT = ctx.enter_context(nc.semaphore("semT"))
        semS = ctx.enter_context(nc.semaphore("semS"))
        semV = ctx.enter_context(nc.semaphore("semV"))
        semG = ctx.enter_context(nc.semaphore("semG"))
        SEM = dict(D=semD, X=semX, O=semO, T=semT, S=semS, V=semV, G=semG)

        plan = {k: [] for k in ("T", "S", "V", "SY", "G")}
        cnt = {"T": 0, "S": 0, "V": 0, "G": 0, "D": 0, "X": 0, "O": 0}

        def op(eng, fn, waits=(), inc=None):
            incs = []
            r = None
            if inc is not None:
                step = 16 if inc in ("D", "X", "O") else 1
                cnt[inc] += step
                r = cnt[inc]
                incs.append((inc, step))
            plan[eng].append((tuple(waits), fn, tuple(incs)))
            return r

        W = lambda ch, val: (ch, val)

        loads = [
            (dl_s, dl_d), (osc_s, osc_d), (qs_s, qs_d), (ks_s, ks_d),
            (g1r_s, g1r_d), (g1i_s, g1i_d), (br_s, br_d), (bi_s, bi_d),
            (selL1_s, selL1_d), (selR1_s, selR1_d), (selL2_s, selL2_d), (selR2_s, selR2_d),
            (dmaxT_s, dmaxT_d), (causT_s, causT_d), (expc_s, expc_d),
            (ident_s, ident_d), (ones1_s, ones1_d),
        ]
        for dst, src in loads:
            op("SY", (lambda e, d=dst, s=src: e.dma_start(d[:], s[:])), inc="D")
        for c4 in range(4):
            op("SY", (lambda e, c=c4: e.dma_start(
                g2r_s[:, c * 16:(c + 1) * 16], bass.AP(g2r_d, c * 256, [[16, 16], [1, 16]]))), inc="D")
            op("SY", (lambda e, c=c4: e.dma_start(
                g2i_s[:, c * 16:(c + 1) * 16], bass.AP(g2i_d, c * 256, [[16, 16], [1, 16]]))), inc="D")
        params_loaded = cnt["D"]

        xload = {}

        def load_x(m, waits=()):
            row0 = (NCH - 1 - m) * C
            src = x_d[row0:row0 + C, :].rearrange("(t p) c -> p t c", p=128)
            dst = xb[m % 3][:].rearrange("p (t c) -> p t c", t=2)
            wts = list(waits) + [W("X", cnt["X"])]
            xload[m] = op("SY", (lambda e, s=src, d=dst: e.dma_start(d, s)), waits=wts, inc="X")

        load_x(0)
        if NCH > 1:
            load_x(1)

        PL = W("D", params_loaded)
        sg_done = op("S", lambda e: e.activation(sgd[:], dl_s[:], AF.Sigmoid),
                     waits=[PL], inc="S")
        op("V", lambda e: e.tensor_scalar_max(sgd[:], sgd[:], 1e-35),
           waits=[PL, W("S", sg_done)], inc="V")
        clamp_done = cnt["V"]
        op("S", lambda e: e.activation(lnd[:], sgd[:], AF.Ln), waits=[W("V", clamp_done)], inc="S")
        op("S", lambda e: e.activation(pj[:], expc_s[:, 0:2], AF.Exp, scale=lnd[:]), inc="S")
        op("S", lambda e: e.activation(dvec[:], expc_s[:, 2:4], AF.Exp, scale=lnd[:]), inc="S")
        op("S", lambda e: e.activation(dCv[:], expc_s[:, 4:5], AF.Exp, scale=lnd[:]), inc="S")
        op("S", lambda e: e.activation(MT[:], dmaxT_s[:], AF.Exp, scale=lnd[:]), inc="S")
        tables_S = cnt["S"]
        op("V", lambda e: e.tensor_scalar_mul(dvec[:], dvec[:], osc_s[:, 0:1]),
           waits=[W("S", tables_S)], inc="V")
        op("V", lambda e: e.tensor_scalar_mul(MT[:], MT[:], osc_s[:, 0:1]), inc="V")
        op("V", lambda e: e.tensor_mul(MT[:], MT[:], causT_s[:]), inc="V")
        op("V", lambda e: e.tensor_mul(qks[:], qs_s[:], ks_s[:]), inc="V")
        tables_V = cnt["V"]
        op("S", lambda e: e.copy(biasbf[0:1, 0:512], br_s[:]), waits=[PL], inc="S")
        op("S", lambda e: e.copy(biasbf[0:1, 512:1024], bi_s[:]), inc="S")

        last_SC_read = cnt["S"]
        last_T1_read = [cnt["V"], cnt["V"]]
        last_T2_read = cnt["V"]
        for w in range(2):
            g1 = (g1r_s, g1i_s)[w]
            g2 = (g2r_s, g2i_s)[w]
            for c4 in range(4):
                tP = op("T", (lambda e, g=g1, c=c4: e.matmul(
                    cps[0][0:32, :], g[:, c * 32:(c + 1) * 32], selR1_s[:], start=True, stop=True)),
                    waits=[W("D", params_loaded), W("S", last_SC_read)], inc="T")
                eP = op("S", (lambda e: e.copy(Ps_s[:], cps[0][0:32, :])), waits=[W("T", tP)], inc="S")
                tP2 = op("T", (lambda e, g=g2, c=c4: e.matmul(
                    cps[0][0:16, :], g[:, c * 16:(c + 1) * 16], selR2_s[:], start=True, stop=True)),
                    waits=[W("S", eP)], inc="T")
                eP2 = op("S", (lambda e: e.copy(P2_s[:], cps[0][0:16, :])), waits=[W("T", tP2)], inc="S")
                last_SC_read = eP2
                tT2 = op("T", (lambda e: e.matmul(cps[0][:], selL2_s[:], P2_s[:], start=True, stop=True)),
                         waits=[W("S", eP2), W("V", last_T2_read)], inc="T")
                eT2 = op("S", (lambda e: e.copy(T2_s[:], cps[0][:])), waits=[W("T", tT2)], inc="S")
                for pt in range(4):
                    tT1 = op("T", (lambda e, p=pt: e.matmul(
                        fps[p % 2][:], selL1_s[:, p * 128:(p + 1) * 128], Ps_s[:], start=True, stop=True)),
                        waits=[W("S", eP), W("V", last_T1_read[pt % 2])], inc="T")
                    if c4 == 0:
                        vm = op("V", (lambda e, p=pt, ww=w: e.tensor_mul(
                            Wacc[ww][:, p * 512:(p + 1) * 512], fps[p % 2][:], T2_s[:])),
                            waits=[W("T", tT1), W("S", eT2)], inc="V")
                    else:
                        op("V", (lambda e, p=pt: e.tensor_mul(sgb[:, 0:512], fps[p % 2][:], T2_s[:])),
                           waits=[W("T", tT1), W("S", eT2)], inc="V")
                        vm = op("V", (lambda e, p=pt, ww=w: e.tensor_add(
                            Wacc[ww][:, p * 512:(p + 1) * 512],
                            Wacc[ww][:, p * 512:(p + 1) * 512], sgb[:, 0:512])), inc="V")
                    last_T1_read[pt % 2] = vm
                    last_T2_read = vm
        wacc_done = cnt["V"]
        op("S", lambda e: e.copy(WTr[:], Wacc[0][:]), waits=[W("V", wacc_done)], inc="S")
        op("S", lambda e: e.copy(WTi[:], Wacc[1][:]), inc="S")
        op("S", lambda e: e.activation(WTiN[:], Wacc[1][:], AF.Identity, scale=-1.0), inc="S")
        w_ready = cnt["S"]

        stat_done = {}
        newton_done = {}
        apply_done = {}
        pk_done = {}

        def ln_stats(m):
            lw = W("X", xload[m])
            pb = m % 3
            for t in range(2):
                op("S", (lambda e, t=t, pb=pb: e.activation(
                    junkbf[:], xb[pb][:, t * 1024:(t + 1) * 1024], AF.Square,
                    accum_out=ssq[:, t:t + 1])), waits=[lw] if t == 0 else (), inc="S")
            for t in range(2):
                op("S", (lambda e, t=t, pb=pb: e.activation(
                    junkbf[:], xb[pb][:, t * 1024:(t + 1) * 1024], AF.Identity,
                    accum_out=ssum[:, t:t + 1])), inc="S")
            stat_done[m] = cnt["S"]

        def ln_stage_a(m):
            op("V", (lambda e: e.tensor_scalar_mul(mu[:], ssum[:], 1.0 / V)),
               waits=[W("S", stat_done[m])], inc="V")
            op("V", lambda e: e.tensor_mul(nt1[:], mu[:], mu[:]), inc="V")
            op("V", (lambda e: e.scalar_tensor_tensor(
                vw[:], ssq[:], 1.0 / V, nt1[:], ALU.mult, ALU.subtract)), inc="V")
            op("V", lambda e: e.tensor_scalar_add(vw[:], vw[:], EPS), inc="V")
            op("V", lambda e: e.tensor_scalar(nt4[:], vw[:], 0.5, 0.5, ALU.mult, ALU.add), inc="V")
            op("V", lambda e: e.reciprocal(yy[:], nt4[:]), inc="V")

        def ln_stage_b(m, xwar=None):
            pa = m % 2
            pb = m % 3
            for it in range(3):
                op("V", lambda e: e.tensor_mul(nt4[:], yy[:], yy[:]), inc="V")
                op("V", (lambda e: e.scalar_tensor_tensor(
                    nt4[:], nt4[:], -0.5, vw[:], ALU.mult, ALU.mult)), inc="V")
                op("V", (lambda e, it=it: e.scalar_tensor_tensor(
                    minv[:] if it == 2 else yy[:], nt4[:], 1.5, yy[:], ALU.add, ALU.mult)), inc="V")
            op("V", (lambda e: e.scalar_tensor_tensor(
                mnb[:], mu[:], -1.0, minv[:], ALU.mult, ALU.mult)), inc="V")
            newton_done[m] = cnt["V"]
            for t in range(2):
                wts = []
                if t == 0:
                    wts.append(W("V", newton_done[m]))
                    if xwar is not None:
                        wts.append(W("T", xwar))
                op("S", (lambda e, t=t, pa=pa, pb=pb: e.activation(
                    Xb[pa][:, t * 1024:(t + 1) * 1024], xb[pb][:, t * 1024:(t + 1) * 1024],
                    AF.Identity, scale=minv[:, t:t + 1], bias=mnb[:, t:t + 1])),
                    waits=wts, inc="S")
            apply_done[m] = cnt["S"]

        def ln_pk(m, xwar=None):
            pa = m % 2
            for t in range(2):
                wts = [W("S", apply_done[m])] if t == 0 else []
                if t == 0 and xwar is not None:
                    wts.append(W("T", xwar))
                op("S", (lambda e, t=t, pa=pa: e.activation(
                    PKb[pa][:, t * 1024:(t + 1) * 1024], Xb[pa][:, t * 1024:(t + 1) * 1024],
                    AF.Identity, scale=pj[:, t:t + 1])), waits=wts, inc="S")
            pk_done[m] = cnt["S"]

        ln_stats(0)
        ln_stage_a(0)
        ln_stage_b(0)
        ln_pk(0)

        eXTv = {}; eQKTv = {}; tA = {}; tB = {}; tF = {}; tD = {}; tC = {}; tE = {}
        silu_done = {}; evacC_done = {}; conv_done = {}; fold_done = {}
        mask_done = {}; combine_done = {}; final_g = {}; sg_g = {}; odma = {}
        DU = [(0, 0), (1, 0), (0, 1), (1, 1)]
        EU = [(u, vh) for u in range(8) for vh in range(2)]
        FU = [(0, 0), (0, 1), (1, 0), (1, 1)]

        def emit_E(n, k):
            pa = n % 2
            u, vh = EU[k]
            bank = eps[k % 2]
            wts = [W("S", pk_done[n])]
            if k >= 2:
                wts.append(W("V", fold_done[n][k - 2]))
            elif (n - 1) in fold_done:
                wts.append(W("V", fold_done[n - 1][14 + (k % 2)]))
            for jb in range(2):
                kw = wts if jb == 0 else ()
                op("T", (lambda e, pa=pa, u=u, vh=vh, jb=jb, bank=bank: e.matmul(
                    bank[:], PKb[pa][:, jb * 1024 + u * 128:jb * 1024 + u * 128 + 128],
                    Xb[pa][:, jb * 1024 + vh * 512:jb * 1024 + vh * 512 + 512],
                    start=(jb == 0), stop=(jb == 1))), waits=kw,
                    inc="T" if jb == 1 else None)
            tE[n][k] = cnt["T"]
            u_off = u * 1024 + vh * 512
            fw = [W("T", tE[n][k])]
            if (n - 1) in conv_done:
                fw.append(W("S", conv_done[n - 1][u]))
            if n == 0:
                fold_done[n][k] = op("V", (lambda e, off=u_off, k=k: e.tensor_copy(
                    Sst[:, off:off + 512], eps[k % 2][:])), waits=fw, inc="V")
            else:
                fold_done[n][k] = op("V", (lambda e, off=u_off, k=k: e.scalar_tensor_tensor(
                    Sst[:, off:off + 512], Sst[:, off:off + 512], dCv[:, 0:1],
                    eps[k % 2][:], ALU.mult, ALU.add)), waits=fw, inc="V")

        for n in range(NCH):
            pa = n % 2
            first, last = (n == 0), (n == NCH - 1)
            tA[n] = {}; tB[n] = {}; tF[n] = {}; tC[n] = {}; tD[n] = {}; tE[n] = {}
            eXTv[n] = {}; eQKTv[n] = {}; fold_done.setdefault(n, {})
            silu_done[n] = {}; evacC_done[n] = {}

            if n + 2 < NCH:
                load_x(n + 2, waits=[W("G", final_g[n - 1])] if n >= 1 else ())

            if n + 1 < NCH:
                ln_stats(n + 1)
            if n >= 1:
                cw = [W("V", fold_done[n - 1][15])]
                if tD[n - 1]:
                    cw.append(W("T", tD[n - 1][3]))
                conv_done[n - 1] = {}
                for u in range(8):
                    off = u * 1024
                    conv_done[n - 1][u] = op("S", (lambda e, off=off: e.copy(
                        Sbf[:, off:off + 1024], Sst[:, off:off + 1024])),
                        waits=cw if u == 0 else (), inc="S")

            for half in range(2):
                for u in (2 * half, 2 * half + 1):
                    wts = []
                    if u == 0:
                        wts.append(W("S", apply_done[n]))
                        wts.append(W("V", eQKTv[n - 1][3]) if n >= 1 else W("S", w_ready))
                    if u == 2 and n >= 1:
                        wts.append(W("V", eQKTv[n - 1][7]))
                    for k in range(4):
                        vb = 2 * u + k // 2
                        jb = k % 2
                        off = (u % 2) * 512 + (vb % 2) * 256 + jb * 128
                        r = op("T", (lambda e, pa=pa, vb=vb, jb=jb, off=off, bank=trps[u // 2]: e.transpose(
                            bank[:, off:off + 128],
                            Xb[pa][:, jb * 1024 + vb * 128:jb * 1024 + vb * 128 + 128],
                            ident_s[:])), waits=wts if k == 0 else (), inc="T")
                    tA[n][u] = r
                eXTv[n][half] = op("V", (lambda e, pa=pa, half=half: e.tensor_copy(
                    XTb[pa][:, half * 1024:(half + 1) * 1024], trps[half][:])),
                    waits=[W("T", tA[n][2 * half + 1])], inc="V")
                for u in (2 * half, 2 * half + 1):
                    for vb in (2 * u, 2 * u + 1):
                        off = (u % 2) * 512 + (vb % 2) * 256
                        eQKTv[n][vb] = op("V", (lambda e, pa=pa, vb=vb, off=off, bank=trps[u // 2]: e.tensor_scalar_mul(
                            QKTb[pa][:, vb * 256:(vb + 1) * 256], bank[:, off:off + 256],
                            qks[:, vb:vb + 1])), inc="V")

            if n + 1 < NCH:
                ln_stage_a(n + 1)

            ei = 0

            def run_E(cnt_e):
                nonlocal ei
                if last:
                    return
                for _ in range(cnt_e):
                    if ei < 16:
                        emit_E(n, ei)
                        ei += 1

            run_E(2)

            for jb in range(2):
                for vb in range(8):
                    kw = [W("V", eQKTv[n][vb]), W("V", eXTv[n][vb // 4])]
                    if jb == 0 and vb == 0:
                        if n >= 1:
                            kw.append(W("V", mask_done[n - 1]))
                            kw.append(W("S", evacC_done[n - 1][2]))
                        else:
                            kw.append(W("S", w_ready))
                    op("T", (lambda e, pa=pa, jb=jb, vb=vb: e.matmul(
                        cps[0][:, jb * 256:(jb + 1) * 256],
                        XTb[pa][:, vb * 256 + jb * 128:vb * 256 + jb * 128 + 128],
                        QKTb[pa][:, vb * 256:(vb + 1) * 256],
                        start=(vb == 0), stop=(vb == 7))), waits=kw,
                        inc="T" if vb == 7 else None)
                tB[n][jb] = cnt["T"]

            for jb in range(2):
                op("V", (lambda e, pa=pa, jb=jb: e.tensor_mul(
                    mkT[pa][:, jb * 256:(jb + 1) * 256], cps[0][:, jb * 256:(jb + 1) * 256],
                    MT[:, jb * 256:(jb + 1) * 256])),
                    waits=[W("T", tB[n][1])], inc="V")
            mask_done[n] = cnt["V"]

            if n + 1 < NCH:
                xwar = tE[n - 1][15] if (n - 1) in tE and tE[n - 1] else None
                ln_stage_b(n + 1, xwar=xwar)
                ln_pk(n + 1, xwar=xwar)

            run_E(2)

            for k, (ib, vh) in enumerate(DU):
                bank = cps[k % 2]
                wts = [W("V", mask_done[n])]
                if k >= 2:
                    wts.append(W("S", evacC_done[n][k - 2]))
                elif n >= 1:
                    wts.append(W("S", evacC_done[n - 1][2 + k]))
                else:
                    wts.append(W("S", w_ready))
                for jb in range(2):
                    kw = wts if jb == 0 else ()
                    op("T", (lambda e, pa=pa, ib=ib, vh=vh, jb=jb, bank=bank: e.matmul(
                        bank[:], mkT[pa][:, jb * 256 + ib * 128:jb * 256 + ib * 128 + 128],
                        Xb[pa][:, jb * 1024 + vh * 512:jb * 1024 + vh * 512 + 512],
                        start=(jb == 0), stop=(jb == 1))), waits=kw,
                        inc="T" if jb == 1 else None)
                tC[n][k] = cnt["T"]
                off = ib * 1024 + vh * 512
                ew = [W("T", tC[n][k])]
                if n >= 1:
                    if combine_done.get(n - 1):
                        ew.append(W("V", combine_done[n - 1][k]))
                    else:
                        ew.append(W("G", sg_g[n - 1]))
                evacC_done[n][k] = op("S", (lambda e, off=off, k=k: e.copy(
                    intra_s[:, off:off + 512], cps[k % 2][:])), waits=ew, inc="S")
                if k == 1:
                    run_E(2)
            run_E(2)

            for ui, (tb, RI) in enumerate(FU):
                bank = fps[ui % 2]
                for k in range(8):
                    vb = k
                    rhs = (WTr if vb < 4 else WTiN) if RI == 0 else (WTi if vb < 4 else WTr)
                    ip = vb % 4
                    kw = [W("V", eXTv[n][vb // 4])]
                    if k == 0 and ui >= 2:
                        kw.append(W("S", silu_done[n][ui - 2]))
                    if k == 0 and ui < 2 and n >= 1:
                        if combine_done.get(n - 1):
                            kw.append(W("V", combine_done[n - 1][2 + (ui % 2)]))
                        else:
                            kw.append(W("S", silu_done[n - 1][2 + (ui % 2)]))
                    op("T", (lambda e, pa=pa, tb=tb, vb=vb, ip=ip, rhs=rhs, bank=bank, k=k: e.matmul(
                        bank[:], XTb[pa][:, vb * 256 + tb * 128:vb * 256 + tb * 128 + 128],
                        rhs[:, ip * 512:(ip + 1) * 512], start=(k == 0), stop=False)),
                        waits=kw, inc=None)
                tF[n][ui] = op("T", (lambda e, RI=RI, bank=bank: e.matmul(
                    bank[:], ones1_s[:], biasbf[0:1, RI * 512:(RI + 1) * 512],
                    start=False, stop=True)), inc="T")
                sw = [W("T", tF[n][ui])]
                if ui == 0 and n >= 1:
                    sw.append(W("G", sg_g[n - 1]))
                silu_done[n][ui] = op("S", (lambda e, tb=tb, RI=RI, ui=ui: e.activation(
                    gb[:, tb * 1024 + RI * 512:tb * 1024 + RI * 512 + 512],
                    fps[ui % 2][:], AF.Silu)), waits=sw, inc="S")
                run_E(2)

            combine_done[n] = {}
            if not first:
                for k, (ib, vh) in enumerate(DU):
                    bank = fps[k % 2]
                    wts = [W("S", conv_done[n - 1][7]),
                           W("S", silu_done[n][2 + (k % 2)])]
                    if k >= 2:
                        wts.append(W("V", combine_done[n][k - 2]))
                    for u in range(8):
                        kw = wts if u == 0 else ()
                        op("T", (lambda e, pa=pa, ib=ib, vh=vh, u=u, bank=bank: e.matmul(
                            bank[:], QKTb[pa][:, u * 256 + ib * 128:u * 256 + ib * 128 + 128],
                            Sbf[:, u * 1024 + vh * 512:u * 1024 + vh * 512 + 512],
                            start=(u == 0), stop=(u == 7))), waits=kw,
                            inc="T" if u == 7 else None)
                    tD[n][k] = cnt["T"]
                    off = ib * 1024 + vh * 512
                    vw_ = [W("T", tD[n][k]), W("S", evacC_done[n][k])]
                    if n >= 1:
                        vw_.append(W("G", sg_g[n - 1]))
                    combine_done[n][k] = op("V", (lambda e, off=off, ib=ib, k=k: e.scalar_tensor_tensor(
                        s_sb[:, off:off + 512], fps[k % 2][:], dvec[:, ib:ib + 1],
                        intra_s[:, off:off + 512], ALU.mult, ALU.add)), waits=vw_, inc="V")

            s_src = intra_s if first else s_sb
            for t in range(2):
                wts = []
                if t == 0:
                    wts.append(W("S", silu_done[n][3]))
                    wts.append(W("S", evacC_done[n][3]) if first else W("V", combine_done[n][3]))
                op("G", (lambda e, t=t, s_src=s_src: e.tensor_mul(
                    sgb[:, t * 1024:(t + 1) * 1024], s_src[:, t * 1024:(t + 1) * 1024],
                    gb[:, t * 1024:(t + 1) * 1024])), waits=wts, inc="G")
            sg_g[n] = cnt["G"]
            for t in range(2):
                wts = []
                if t == 0 and n >= 2:
                    wts.append(W("O", odma[n - 2]))
                op("G", (lambda e, t=t, pa=pa, pb=n % 3: e.tensor_add(
                    ob[pa][:, t * 1024:(t + 1) * 1024], sgb[:, t * 1024:(t + 1) * 1024],
                    xb[pb][:, t * 1024:(t + 1) * 1024])), waits=wts, inc="G")
            final_g[n] = cnt["G"]

            row0 = (NCH - 1 - n) * C
            dst = y_d[row0:row0 + C, :].rearrange("(t p) c -> p t c", p=128)
            src = ob[pa][:].rearrange("p (t c) -> p t c", t=2)
            odma[n] = op("SY", (lambda e, d=dst, s=src: e.dma_start(d, s)),
                         waits=[W("G", final_g[n]), W("O", cnt["O"])], inc="O")

        plan["SY"].append(((W("O", cnt["O"]),), None, ()))

        def emit(eng, key, selfch=None):
            done = 0
            mark = {}
            for waits, fn, incs in plan[key]:
                if selfch is not None and incs and done > mark.get(selfch, -1):
                    eng.wait_ge(SEM[selfch], done)
                    mark[selfch] = done
                for (ch, val) in waits:
                    if val > mark.get(ch, -1):
                        eng.wait_ge(SEM[ch], val)
                        mark[ch] = val
                if fn is None:
                    continue
                ins = fn(eng)
                for (ch, step) in incs:
                    ins = ins.then_inc(SEM[ch], step)
                    if ch == selfch:
                        done += step

        with nc.Block() as block:
            @block.sync
            def _(e):
                emit(e, "SY")

            @block.tensor
            def _(e):
                emit(e, "T")

            @block.scalar
            def _(e):
                emit(e, "S", selfch="S")

            @block.vector
            def _(e):
                emit(e, "V", selfch="V")

            @block.gpsimd
            def _(e):
                emit(e, "G", selfch="G")

    return nc


def make_inmap(xb, dl, osc, qsv, ksv, G1R, G2R, G1I, G2I, brv, biv):
    cs = host_consts()
    return {
        "x": np.ascontiguousarray(xb, np.float32),
        "dl": np.full((128, 1), dl, np.float32),
        "osc": np.full((128, 1), osc, np.float32),
        "qs": np.ascontiguousarray(np.asarray(qsv, np.float32).reshape(8, 128).T),
        "ks": np.ascontiguousarray(np.asarray(ksv, np.float32).reshape(8, 128).T),
        "g1r": np.ascontiguousarray(np.asarray(G1R, np.float32).reshape(32, 128)),
        "g1i": np.ascontiguousarray(np.asarray(G1I, np.float32).reshape(32, 128)),
        "g2r": np.ascontiguousarray(np.asarray(G2R, np.float32).reshape(4, 256)),
        "g2i": np.ascontiguousarray(np.asarray(G2I, np.float32).reshape(4, 256)),
        "br": np.ascontiguousarray(np.asarray(brv, np.float32).reshape(1, 512)),
        "bi": np.ascontiguousarray(np.asarray(biv, np.float32).reshape(1, 512)),
        "selL1": cs["selL1"], "selR1": cs["selR1"], "selL2": cs["selL2"], "selR2": cs["selR2"],
        "dmaxT": cs["dmaxT"], "causT": cs["causT"], "expc": cs["expc"],
        "ident": cs["ident"], "ones1": cs["ones1"],
    }




_CACHE = {}


def kernel(L0_attn_x, L0_attn_decay_logit, L0_attn_out_scale, L0_attn_q_scale,
           L0_attn_k_scale, L0_ffn_G1_R, L0_ffn_G2_R, L0_ffn_G1_I, L0_ffn_G2_I,
           L0_ffn_bias_real, L0_ffn_bias_imag):
    x = np.ascontiguousarray(np.asarray(L0_attn_x, np.float32))
    B, T, _ = x.shape
    if T not in _CACHE:
        _CACHE[T] = build(T)
    nc = _CACHE[T]
    shared = make_inmap(x[0], np.float32(L0_attn_decay_logit), np.float32(L0_attn_out_scale),
                        L0_attn_q_scale, L0_attn_k_scale, L0_ffn_G1_R, L0_ffn_G2_R,
                        L0_ffn_G1_I, L0_ffn_G2_I, L0_ffn_bias_real, L0_ffn_bias_imag)
    in_maps = []
    for b in range(B):
        m = dict(shared)
        m["x"] = np.ascontiguousarray(x[b])
        in_maps.append(m)
    res = run_bass_kernel_spmd(nc, in_maps, core_ids=list(range(B)))
    return np.stack([np.asarray(r["y"], np.float32) for r in res.results], axis=0)


# version 36 (final)
# speedup vs baseline: 1.1429x; 1.1429x over previous
import sys
for _p in ("/opt/trn_rl_repo",):
    if _p not in sys.path:
        sys.path.append(_p)
from contextlib import ExitStack
import numpy as np
import ml_dtypes
import concourse.bass as bass
import concourse.mybir as mybir
from concourse.bass_utils import run_bass_kernel_spmd

F32 = mybir.dt.float32
BF16 = mybir.dt.bfloat16
AF = mybir.ActivationFunctionType
ALU = mybir.AluOpType
AX = mybir.AxisListType
BF = ml_dtypes.bfloat16

C = 256
V = 1024
EPS = 1e-5

def host_consts():
    p = np.arange(128)
    i256 = np.arange(256)
    selL1 = np.zeros((32, 512), np.float32)
    P = np.arange(512)
    selL1[8 * (P // 128) + (P % 128) // 16, P] = 1.0
    selR1 = np.zeros((32, 512), np.float32)
    f = np.arange(512)
    selR1[f // 16, f] = 1.0
    selL2 = np.zeros((16, 128), np.float32)
    selL2[p % 16, p] = 1.0
    selR2 = np.zeros((16, 512), np.float32)
    selR2[f % 16, f] = 1.0
    dmaxT = np.zeros((128, 512), np.float32)
    causT = np.zeros((128, 512), np.float32)
    for jb in range(2):
        j = jb * 128 + p[:, None]
        dmaxT[:, jb * 256:(jb + 1) * 256] = np.maximum(j - i256[None, :] - 1, 0)
        causT[:, jb * 256:(jb + 1) * 256] = (j > i256[None, :])
    expc = np.zeros((128, 8), np.float32)
    expc[:, 0] = p
    expc[:, 1] = 128 + p
    expc[:, 2] = 255 - p
    expc[:, 3] = 127 - p
    expc[:, 4] = 256.0
    ident = np.eye(128, dtype=BF)
    ones1 = np.ones((1, 128), dtype=BF)
    return dict(selL1=selL1, selR1=selR1, selL2=selL2, selR2=selR2,
                dmaxT=dmaxT, causT=causT, expc=expc, ident=ident, ones1=ones1)


def build(T):
    NCH = T // C
    nc = bass.Bass("TRN2", target_bir_lowering=False)

    x_d = nc.dram_tensor("x", [T, V], F32, kind="ExternalInput")
    y_d = nc.dram_tensor("y", [T, V], F32, kind="ExternalOutput")
    dl_d = nc.dram_tensor("dl", [128, 1], F32, kind="ExternalInput")
    osc_d = nc.dram_tensor("osc", [128, 1], F32, kind="ExternalInput")
    qs_d = nc.dram_tensor("qs", [128, 8], F32, kind="ExternalInput")
    ks_d = nc.dram_tensor("ks", [128, 8], F32, kind="ExternalInput")
    g1r_d = nc.dram_tensor("g1r", [32, 128], F32, kind="ExternalInput")
    g1i_d = nc.dram_tensor("g1i", [32, 128], F32, kind="ExternalInput")
    g2r_d = nc.dram_tensor("g2r", [4, 256], F32, kind="ExternalInput")
    g2i_d = nc.dram_tensor("g2i", [4, 256], F32, kind="ExternalInput")
    br_d = nc.dram_tensor("br", [1, 512], F32, kind="ExternalInput")
    bi_d = nc.dram_tensor("bi", [1, 512], F32, kind="ExternalInput")
    selL1_d = nc.dram_tensor("selL1", [32, 512], F32, kind="ExternalInput")
    selR1_d = nc.dram_tensor("selR1", [32, 512], F32, kind="ExternalInput")
    selL2_d = nc.dram_tensor("selL2", [16, 128], F32, kind="ExternalInput")
    selR2_d = nc.dram_tensor("selR2", [16, 512], F32, kind="ExternalInput")
    dmaxT_d = nc.dram_tensor("dmaxT", [128, 512], F32, kind="ExternalInput")
    causT_d = nc.dram_tensor("causT", [128, 512], F32, kind="ExternalInput")
    expc_d = nc.dram_tensor("expc", [128, 8], F32, kind="ExternalInput")
    ident_d = nc.dram_tensor("ident", [128, 128], BF16, kind="ExternalInput")
    ones1_d = nc.dram_tensor("ones1", [1, 128], BF16, kind="ExternalInput")

    ctx = ExitStack()
    with ctx:
        sb = lambda name, shape, dt=F32: ctx.enter_context(nc.sbuf_tensor(name, shape, dt))
        ps = lambda name, shape, dt=F32: ctx.enter_context(nc.psum_tensor(name, shape, dt))
        xb = [sb("xb0", [128, 2048]), sb("xb1", [128, 2048]), sb("xb2", [128, 2048])]
        Xb = [sb("Xb0", [128, 2048], BF16), sb("Xb1", [128, 2048], BF16)]
        XTb = [sb("XT0", [128, 2048], BF16), sb("XT1", [128, 2048], BF16)]
        QKTb = [sb("QKT0", [128, 2048], BF16), sb("QKT1", [128, 2048], BF16)]
        PKb = [sb("PK0", [128, 2048], BF16), sb("PK1", [128, 2048], BF16)]
        mkT = [sb("mkT0", [128, 512], BF16), sb("mkT1", [128, 512], BF16)]
        intra_s = sb("intra_s", [128, 2048])
        s_sb = sb("s_sb", [128, 2048])
        gb = sb("gb", [128, 2048])
        sgb = sb("sgb", [128, 2048])
        ob = [sb("ob0", [128, 2048]), sb("ob1", [128, 2048])]
        Sst = sb("Sst", [128, 8192])
        Sbf = sb("Sbf", [128, 8192], BF16)
        WTr = sb("WTr", [128, 2048], BF16)
        WTi = sb("WTi", [128, 2048], BF16)
        WTiN = sb("WTiN", [128, 2048], BF16)
        MT = sb("MT", [128, 512])
        biasbf = sb("biasbf", [1, 1024], BF16)
        dl_s = sb("dl_s", [128, 1]); osc_s = sb("osc_s", [128, 1])
        qs_s = sb("qs_s", [128, 8]); ks_s = sb("ks_s", [128, 8]); qks = sb("qks", [128, 8])
        g1r_s = sb("g1r_s", [32, 128]); g1i_s = sb("g1i_s", [32, 128])
        g2r_s = sb("g2r_s", [16, 64]); g2i_s = sb("g2i_s", [16, 64])
        br_s = sb("br_s", [1, 512]); bi_s = sb("bi_s", [1, 512])
        selL1_s = sb("selL1_s", [32, 512]); selR1_s = sb("selR1_s", [32, 512])
        selL2_s = sb("selL2_s", [16, 128]); selR2_s = sb("selR2_s", [16, 512])
        dmaxT_s = sb("dmaxT_s", [128, 512]); causT_s = sb("causT_s", [128, 512])
        expc_s = sb("expc_s", [128, 8])
        ident_s = sb("ident_s", [128, 128], BF16)
        ones1_s = sb("ones1_s", [1, 128], BF16)
        sgd = sb("sgd", [128, 1]); lnd = sb("lnd", [128, 1])
        pj = sb("pj", [128, 2]); dvec = sb("dvec", [128, 2]); dCv = sb("dCv", [128, 1])
        ssq = sb("ssq", [128, 2]); ssum = sb("ssum", [128, 2])
        mu = sb("mu", [128, 2]); nt1 = sb("nt1", [128, 2]); vw = sb("vw", [128, 2])
        yy = sb("yy", [128, 2]); nt4 = sb("nt4", [128, 2])
        minv = sb("minv", [128, 2]); mnb = sb("mnb", [128, 2])
        junkbf = sb("junkbf", [128, 1024], BF16)
        Ps_s = sb("Ps_s", [32, 512]); P2_s = sb("P2_s", [16, 512]); T2_s = sb("T2_s", [128, 512])
        Wacc = ob

        trps = [ps("trps0", [128, 1024], BF16),
                ps("trps1", [128, 1024], BF16)]
        fps = [ps("fps0", [128, 512]), ps("fps1", [128, 512])]
        cps = [ps("cps0", [128, 512]), ps("cps1", [128, 512])]
        eps = [ps("eps0", [128, 512]), ps("eps1", [128, 512])]

        semD = ctx.enter_context(nc.semaphore("semD"))
        semX = ctx.enter_context(nc.semaphore("semX"))
        semO = ctx.enter_context(nc.semaphore("semO"))
        semT = ctx.enter_context(nc.semaphore("semT"))
        semS = ctx.enter_context(nc.semaphore("semS"))
        semV = ctx.enter_context(nc.semaphore("semV"))
        semG = ctx.enter_context(nc.semaphore("semG"))
        SEM = dict(D=semD, X=semX, O=semO, T=semT, S=semS, V=semV, G=semG)

        plan = {k: [] for k in ("T", "S", "V", "SY", "G")}
        cnt = {"T": 0, "S": 0, "V": 0, "G": 0, "D": 0, "X": 0, "O": 0}

        def op(eng, fn, waits=(), inc=None):
            incs = []
            r = None
            if inc is not None:
                step = 16 if inc in ("D", "X", "O") else 1
                cnt[inc] += step
                r = cnt[inc]
                incs.append((inc, step))
            plan[eng].append((tuple(waits), fn, tuple(incs)))
            return r

        W = lambda ch, val: (ch, val)

        loads = [
            (dl_s, dl_d), (osc_s, osc_d), (qs_s, qs_d), (ks_s, ks_d),
            (g1r_s, g1r_d), (g1i_s, g1i_d), (br_s, br_d), (bi_s, bi_d),
            (selL1_s, selL1_d), (selR1_s, selR1_d), (selL2_s, selL2_d), (selR2_s, selR2_d),
            (dmaxT_s, dmaxT_d), (causT_s, causT_d), (expc_s, expc_d),
            (ident_s, ident_d), (ones1_s, ones1_d),
        ]
        for dst, src in loads:
            op("SY", (lambda e, d=dst, s=src: e.dma_start(d[:], s[:])), inc="D")
        for c4 in range(4):
            op("SY", (lambda e, c=c4: e.dma_start(
                g2r_s[:, c * 16:(c + 1) * 16], bass.AP(g2r_d, c * 256, [[16, 16], [1, 16]]))), inc="D")
            op("SY", (lambda e, c=c4: e.dma_start(
                g2i_s[:, c * 16:(c + 1) * 16], bass.AP(g2i_d, c * 256, [[16, 16], [1, 16]]))), inc="D")
        params_loaded = cnt["D"]

        xload = {}

        def load_x(m, waits=()):
            row0 = (NCH - 1 - m) * C
            src = x_d[row0:row0 + C, :].rearrange("(t p) c -> p t c", p=128)
            dst = xb[m % 3][:].rearrange("p (t c) -> p t c", t=2)
            wts = list(waits) + [W("X", cnt["X"])]
            xload[m] = op("SY", (lambda e, s=src, d=dst: e.dma_start(d, s)), waits=wts, inc="X")

        load_x(0)
        if NCH > 1:
            load_x(1)

        PL = W("D", params_loaded)
        sg_done = op("S", lambda e: e.activation(sgd[:], dl_s[:], AF.Sigmoid),
                     waits=[PL], inc="S")
        op("V", lambda e: e.tensor_scalar_max(sgd[:], sgd[:], 1e-35),
           waits=[PL, W("S", sg_done)], inc="V")
        clamp_done = cnt["V"]
        op("S", lambda e: e.activation(lnd[:], sgd[:], AF.Ln), waits=[W("V", clamp_done)], inc="S")
        op("S", lambda e: e.activation(pj[:], expc_s[:, 0:2], AF.Exp, scale=lnd[:]), inc="S")
        op("S", lambda e: e.activation(dvec[:], expc_s[:, 2:4], AF.Exp, scale=lnd[:]), inc="S")
        op("S", lambda e: e.activation(dCv[:], expc_s[:, 4:5], AF.Exp, scale=lnd[:]), inc="S")
        op("S", lambda e: e.activation(MT[:], dmaxT_s[:], AF.Exp, scale=lnd[:]), inc="S")
        tables_S = cnt["S"]
        op("V", lambda e: e.tensor_scalar_mul(dvec[:], dvec[:], osc_s[:, 0:1]),
           waits=[W("S", tables_S)], inc="V")
        op("V", lambda e: e.tensor_scalar_mul(MT[:], MT[:], osc_s[:, 0:1]), inc="V")
        op("V", lambda e: e.tensor_mul(MT[:], MT[:], causT_s[:]), inc="V")
        op("V", lambda e: e.tensor_mul(qks[:], qs_s[:], ks_s[:]), inc="V")
        tables_V = cnt["V"]
        op("S", lambda e: e.copy(biasbf[0:1, 0:512], br_s[:]), waits=[PL], inc="S")
        op("S", lambda e: e.copy(biasbf[0:1, 512:1024], bi_s[:]), inc="S")

        last_SC_read = cnt["S"]
        last_T1_read = [cnt["V"], cnt["V"]]
        last_T2_read = cnt["V"]
        for w in range(2):
            g1 = (g1r_s, g1i_s)[w]
            g2 = (g2r_s, g2i_s)[w]
            for c4 in range(4):
                tP = op("T", (lambda e, g=g1, c=c4: e.matmul(
                    cps[0][0:32, :], g[:, c * 32:(c + 1) * 32], selR1_s[:], start=True, stop=True)),
                    waits=[W("D", params_loaded), W("S", last_SC_read)], inc="T")
                eP = op("S", (lambda e: e.copy(Ps_s[:], cps[0][0:32, :])), waits=[W("T", tP)], inc="S")
                tP2 = op("T", (lambda e, g=g2, c=c4: e.matmul(
                    cps[0][0:16, :], g[:, c * 16:(c + 1) * 16], selR2_s[:], start=True, stop=True)),
                    waits=[W("S", eP)], inc="T")
                eP2 = op("S", (lambda e: e.copy(P2_s[:], cps[0][0:16, :])), waits=[W("T", tP2)], inc="S")
                last_SC_read = eP2
                tT2 = op("T", (lambda e: e.matmul(cps[0][:], selL2_s[:], P2_s[:], start=True, stop=True)),
                         waits=[W("S", eP2), W("V", last_T2_read)], inc="T")
                eT2 = op("S", (lambda e: e.copy(T2_s[:], cps[0][:])), waits=[W("T", tT2)], inc="S")
                for pt in range(4):
                    tT1 = op("T", (lambda e, p=pt: e.matmul(
                        fps[p % 2][:], selL1_s[:, p * 128:(p + 1) * 128], Ps_s[:], start=True, stop=True)),
                        waits=[W("S", eP), W("V", last_T1_read[pt % 2])], inc="T")
                    if c4 == 0:
                        vm = op("V", (lambda e, p=pt, ww=w: e.tensor_mul(
                            Wacc[ww][:, p * 512:(p + 1) * 512], fps[p % 2][:], T2_s[:])),
                            waits=[W("T", tT1), W("S", eT2)], inc="V")
                    else:
                        op("V", (lambda e, p=pt: e.tensor_mul(sgb[:, 0:512], fps[p % 2][:], T2_s[:])),
                           waits=[W("T", tT1), W("S", eT2)], inc="V")
                        vm = op("V", (lambda e, p=pt, ww=w: e.tensor_add(
                            Wacc[ww][:, p * 512:(p + 1) * 512],
                            Wacc[ww][:, p * 512:(p + 1) * 512], sgb[:, 0:512])), inc="V")
                    last_T1_read[pt % 2] = vm
                    last_T2_read = vm
        wacc_done = cnt["V"]
        op("S", lambda e: e.copy(WTr[:], Wacc[0][:]), waits=[W("V", wacc_done)], inc="S")
        op("S", lambda e: e.copy(WTi[:], Wacc[1][:]), inc="S")
        op("S", lambda e: e.activation(WTiN[:], Wacc[1][:], AF.Identity, scale=-1.0), inc="S")
        w_ready = cnt["S"]

        stat_done = {}
        newton_done = {}
        apply_done = {}
        pk_done = {}

        def ln_stats(m):
            lw = W("X", xload[m])
            pb = m % 3
            for t in range(2):
                op("S", (lambda e, t=t, pb=pb: e.activation(
                    junkbf[:], xb[pb][:, t * 1024:(t + 1) * 1024], AF.Square,
                    accum_out=ssq[:, t:t + 1])), waits=[lw] if t == 0 else (), inc="S")
            for t in range(2):
                op("S", (lambda e, t=t, pb=pb: e.activation(
                    junkbf[:], xb[pb][:, t * 1024:(t + 1) * 1024], AF.Identity,
                    accum_out=ssum[:, t:t + 1])), inc="S")
            stat_done[m] = cnt["S"]

        def ln_stage(m, xwar=None):
            pa = m % 2
            pb = m % 3
            op("V", (lambda e: e.tensor_scalar_mul(mu[:], ssum[:], 1.0 / V)),
               waits=[W("S", stat_done[m])], inc="V")
            op("V", lambda e: e.tensor_mul(nt1[:], mu[:], mu[:]), inc="V")
            op("V", (lambda e: e.scalar_tensor_tensor(
                vw[:], ssq[:], 1.0 / V, nt1[:], ALU.mult, ALU.subtract)), inc="V")
            op("V", lambda e: e.tensor_scalar_add(vw[:], vw[:], EPS), inc="V")
            op("V", lambda e: e.tensor_scalar(nt4[:], vw[:], 0.5, 0.5, ALU.mult, ALU.add), inc="V")
            op("V", lambda e: e.reciprocal(yy[:], nt4[:]), inc="V")
            for it in range(2):
                op("V", lambda e: e.tensor_mul(nt4[:], yy[:], yy[:]), inc="V")
                op("V", (lambda e: e.scalar_tensor_tensor(
                    nt4[:], nt4[:], -0.5, vw[:], ALU.mult, ALU.mult)), inc="V")
                op("V", (lambda e, it=it: e.scalar_tensor_tensor(
                    minv[:] if it == 1 else yy[:], nt4[:], 1.5, yy[:], ALU.add, ALU.mult)), inc="V")
            op("V", (lambda e: e.scalar_tensor_tensor(
                mnb[:], mu[:], -1.0, minv[:], ALU.mult, ALU.mult)), inc="V")
            newton_done[m] = cnt["V"]
            for t in range(2):
                wts = []
                if t == 0:
                    wts.append(W("V", newton_done[m]))
                    if xwar is not None:
                        wts.append(W("T", xwar))
                op("S", (lambda e, t=t, pa=pa, pb=pb: e.activation(
                    Xb[pa][:, t * 1024:(t + 1) * 1024], xb[pb][:, t * 1024:(t + 1) * 1024],
                    AF.Identity, scale=minv[:, t:t + 1], bias=mnb[:, t:t + 1])),
                    waits=wts, inc="S")
            apply_done[m] = cnt["S"]

        def ln_pk(m, xwar=None):
            pa = m % 2
            for t in range(2):
                wts = [W("S", apply_done[m])] if t == 0 else []
                if t == 0 and xwar is not None:
                    wts.append(W("T", xwar))
                op("S", (lambda e, t=t, pa=pa: e.activation(
                    PKb[pa][:, t * 1024:(t + 1) * 1024], Xb[pa][:, t * 1024:(t + 1) * 1024],
                    AF.Identity, scale=pj[:, t:t + 1])), waits=wts, inc="S")
            pk_done[m] = cnt["S"]

        ln_stats(0)
        ln_stage(0)
        ln_pk(0)

        eXTv = {}; eQKTv = {}; tA = {}; tB = {}; tF = {}; tD = {}; tC = {}; tE = {}
        silu_done = {}; evacC_done = {}; conv_done = {}; fold_done = {}
        mask_done = {}; combine_done = {}; final_g = {}; sg_g = {}; odma = {}
        DU = [(0, 0), (1, 0), (0, 1), (1, 1)]
        EU = [(u, vh) for u in range(8) for vh in range(2)]
        FU = [(0, 0), (0, 1), (1, 0), (1, 1)]

        def emit_E(n, k):
            pa = n % 2
            u, vh = EU[k]
            bank = eps[k % 2]
            wts = [W("S", pk_done[n])]
            if k >= 2:
                wts.append(W("V", fold_done[n][k - 2]))
            elif (n - 1) in fold_done:
                wts.append(W("V", fold_done[n - 1][14 + (k % 2)]))
            for jb in range(2):
                kw = wts if jb == 0 else ()
                op("T", (lambda e, pa=pa, u=u, vh=vh, jb=jb, bank=bank: e.matmul(
                    bank[:], PKb[pa][:, jb * 1024 + u * 128:jb * 1024 + u * 128 + 128],
                    Xb[pa][:, jb * 1024 + vh * 512:jb * 1024 + vh * 512 + 512],
                    start=(jb == 0), stop=(jb == 1))), waits=kw,
                    inc="T" if jb == 1 else None)
            tE[n][k] = cnt["T"]
            u_off = u * 1024 + vh * 512
            fw = [W("T", tE[n][k])]
            if (n - 1) in conv_done:
                fw.append(W("S", conv_done[n - 1][u]))
            if n == 0:
                fold_done[n][k] = op("V", (lambda e, off=u_off, k=k: e.tensor_copy(
                    Sst[:, off:off + 512], eps[k % 2][:])), waits=fw, inc="V")
            else:
                fold_done[n][k] = op("V", (lambda e, off=u_off, k=k: e.scalar_tensor_tensor(
                    Sst[:, off:off + 512], Sst[:, off:off + 512], dCv[:, 0:1],
                    eps[k % 2][:], ALU.mult, ALU.add)), waits=fw, inc="V")

        for n in range(NCH):
            pa = n % 2
            first, last = (n == 0), (n == NCH - 1)
            tA[n] = {}; tB[n] = {}; tF[n] = {}; tC[n] = {}; tD[n] = {}; tE[n] = {}
            eXTv[n] = {}; eQKTv[n] = {}; fold_done.setdefault(n, {})
            silu_done[n] = {}; evacC_done[n] = {}

            if n + 2 < NCH:
                load_x(n + 2, waits=[W("G", final_g[n - 1])] if n >= 1 else ())

            if n + 1 < NCH:
                ln_stats(n + 1)
            if n >= 1:
                cw = [W("V", fold_done[n - 1][15])]
                if tD[n - 1]:
                    cw.append(W("T", tD[n - 1][3]))
                conv_done[n - 1] = {}
                for u in range(8):
                    off = u * 1024
                    conv_done[n - 1][u] = op("S", (lambda e, off=off: e.copy(
                        Sbf[:, off:off + 1024], Sst[:, off:off + 1024])),
                        waits=cw if u == 0 else (), inc="S")

            for half in range(2):
                for u in (2 * half, 2 * half + 1):
                    wts = []
                    if u == 0:
                        wts.append(W("S", apply_done[n]))
                        wts.append(W("V", eQKTv[n - 1][3]) if n >= 1 else W("S", w_ready))
                    if u == 2 and n >= 1:
                        wts.append(W("V", eQKTv[n - 1][7]))
                    for k in range(4):
                        vb = 2 * u + k // 2
                        jb = k % 2
                        off = (u % 2) * 512 + (vb % 2) * 256 + jb * 128
                        r = op("T", (lambda e, pa=pa, vb=vb, jb=jb, off=off, bank=trps[u // 2]: e.transpose(
                            bank[:, off:off + 128],
                            Xb[pa][:, jb * 1024 + vb * 128:jb * 1024 + vb * 128 + 128],
                            ident_s[:])), waits=wts if k == 0 else (), inc="T")
                    tA[n][u] = r
                eXTv[n][half] = op("V", (lambda e, pa=pa, half=half: e.tensor_copy(
                    XTb[pa][:, half * 1024:(half + 1) * 1024], trps[half][:])),
                    waits=[W("T", tA[n][2 * half + 1])], inc="V")
                for u in (2 * half, 2 * half + 1):
                    for vb in (2 * u, 2 * u + 1):
                        off = (u % 2) * 512 + (vb % 2) * 256
                        eQKTv[n][vb] = op("V", (lambda e, pa=pa, vb=vb, off=off, bank=trps[u // 2]: e.tensor_scalar_mul(
                            QKTb[pa][:, vb * 256:(vb + 1) * 256], bank[:, off:off + 256],
                            qks[:, vb:vb + 1])), inc="V")

            if n + 1 < NCH:
                xwar = tE[n - 1][15] if (n - 1) in tE and tE[n - 1] else None
                ln_stage(n + 1, xwar=xwar)
                ln_pk(n + 1, xwar=xwar)

            ei = 0

            def run_E(cnt_e):
                nonlocal ei
                if last:
                    return
                for _ in range(cnt_e):
                    if ei < 16:
                        emit_E(n, ei)
                        ei += 1

            run_E(2)

            for jb in range(2):
                for vb in range(8):
                    kw = [W("V", eQKTv[n][vb]), W("V", eXTv[n][vb // 4])]
                    if jb == 0 and vb == 0:
                        if n >= 1:
                            kw.append(W("V", mask_done[n - 1]))
                            kw.append(W("S", evacC_done[n - 1][2]))
                        else:
                            kw.append(W("S", w_ready))
                    op("T", (lambda e, pa=pa, jb=jb, vb=vb: e.matmul(
                        cps[0][:, jb * 256:(jb + 1) * 256],
                        XTb[pa][:, vb * 256 + jb * 128:vb * 256 + jb * 128 + 128],
                        QKTb[pa][:, vb * 256:(vb + 1) * 256],
                        start=(vb == 0), stop=(vb == 7))), waits=kw,
                        inc="T" if vb == 7 else None)
                tB[n][jb] = cnt["T"]

            for jb in range(2):
                op("V", (lambda e, pa=pa, jb=jb: e.tensor_mul(
                    mkT[pa][:, jb * 256:(jb + 1) * 256], cps[0][:, jb * 256:(jb + 1) * 256],
                    MT[:, jb * 256:(jb + 1) * 256])),
                    waits=[W("T", tB[n][1])], inc="V")
            mask_done[n] = cnt["V"]

            run_E(2)

            for k, (ib, vh) in enumerate(DU):
                bank = cps[k % 2]
                wts = [W("V", mask_done[n])]
                if k >= 2:
                    wts.append(W("S", evacC_done[n][k - 2]))
                elif n >= 1:
                    wts.append(W("S", evacC_done[n - 1][2 + k]))
                else:
                    wts.append(W("S", w_ready))
                for jb in range(2):
                    kw = wts if jb == 0 else ()
                    op("T", (lambda e, pa=pa, ib=ib, vh=vh, jb=jb, bank=bank: e.matmul(
                        bank[:], mkT[pa][:, jb * 256 + ib * 128:jb * 256 + ib * 128 + 128],
                        Xb[pa][:, jb * 1024 + vh * 512:jb * 1024 + vh * 512 + 512],
                        start=(jb == 0), stop=(jb == 1))), waits=kw,
                        inc="T" if jb == 1 else None)
                tC[n][k] = cnt["T"]
                off = ib * 1024 + vh * 512
                ew = [W("T", tC[n][k])]
                if n >= 1:
                    if combine_done.get(n - 1):
                        ew.append(W("V", combine_done[n - 1][k]))
                    else:
                        ew.append(W("G", sg_g[n - 1]))
                evacC_done[n][k] = op("S", (lambda e, off=off, k=k: e.copy(
                    intra_s[:, off:off + 512], cps[k % 2][:])), waits=ew, inc="S")
                if k == 1:
                    run_E(2)
            run_E(2)

            for ui, (tb, RI) in enumerate(FU):
                bank = fps[ui % 2]
                for k in range(8):
                    vb = k
                    rhs = (WTr if vb < 4 else WTiN) if RI == 0 else (WTi if vb < 4 else WTr)
                    ip = vb % 4
                    kw = [W("V", eXTv[n][vb // 4])]
                    if k == 0 and ui >= 2:
                        kw.append(W("S", silu_done[n][ui - 2]))
                    if k == 0 and ui < 2 and n >= 1:
                        if combine_done.get(n - 1):
                            kw.append(W("V", combine_done[n - 1][2 + (ui % 2)]))
                        else:
                            kw.append(W("S", silu_done[n - 1][2 + (ui % 2)]))
                    op("T", (lambda e, pa=pa, tb=tb, vb=vb, ip=ip, rhs=rhs, bank=bank, k=k: e.matmul(
                        bank[:], XTb[pa][:, vb * 256 + tb * 128:vb * 256 + tb * 128 + 128],
                        rhs[:, ip * 512:(ip + 1) * 512], start=(k == 0), stop=False)),
                        waits=kw, inc=None)
                tF[n][ui] = op("T", (lambda e, RI=RI, bank=bank: e.matmul(
                    bank[:], ones1_s[:], biasbf[0:1, RI * 512:(RI + 1) * 512],
                    start=False, stop=True)), inc="T")
                sw = [W("T", tF[n][ui])]
                if ui == 0 and n >= 1:
                    sw.append(W("G", sg_g[n - 1]))
                silu_done[n][ui] = op("S", (lambda e, tb=tb, RI=RI, ui=ui: e.activation(
                    gb[:, tb * 1024 + RI * 512:tb * 1024 + RI * 512 + 512],
                    fps[ui % 2][:], AF.Silu)), waits=sw, inc="S")
                run_E(2)

            combine_done[n] = {}
            if not first:
                for k, (ib, vh) in enumerate(DU):
                    bank = fps[k % 2]
                    wts = [W("S", conv_done[n - 1][7]),
                           W("S", silu_done[n][2 + (k % 2)])]
                    if k >= 2:
                        wts.append(W("V", combine_done[n][k - 2]))
                    for u in range(8):
                        kw = wts if u == 0 else ()
                        op("T", (lambda e, pa=pa, ib=ib, vh=vh, u=u, bank=bank: e.matmul(
                            bank[:], QKTb[pa][:, u * 256 + ib * 128:u * 256 + ib * 128 + 128],
                            Sbf[:, u * 1024 + vh * 512:u * 1024 + vh * 512 + 512],
                            start=(u == 0), stop=(u == 7))), waits=kw,
                            inc="T" if u == 7 else None)
                    tD[n][k] = cnt["T"]
                    off = ib * 1024 + vh * 512
                    vw_ = [W("T", tD[n][k]), W("S", evacC_done[n][k])]
                    if n >= 1:
                        vw_.append(W("G", sg_g[n - 1]))
                    combine_done[n][k] = op("V", (lambda e, off=off, ib=ib, k=k: e.scalar_tensor_tensor(
                        s_sb[:, off:off + 512], fps[k % 2][:], dvec[:, ib:ib + 1],
                        intra_s[:, off:off + 512], ALU.mult, ALU.add)), waits=vw_, inc="V")

            s_src = intra_s if first else s_sb
            for t in range(2):
                wts = []
                if t == 0:
                    wts.append(W("S", silu_done[n][3]))
                    wts.append(W("S", evacC_done[n][3]) if first else W("V", combine_done[n][3]))
                op("G", (lambda e, t=t, s_src=s_src: e.tensor_mul(
                    sgb[:, t * 1024:(t + 1) * 1024], s_src[:, t * 1024:(t + 1) * 1024],
                    gb[:, t * 1024:(t + 1) * 1024])), waits=wts, inc="G")
            sg_g[n] = cnt["G"]
            for t in range(2):
                wts = []
                if t == 0 and n >= 2:
                    wts.append(W("O", odma[n - 2]))
                op("G", (lambda e, t=t, pa=pa, pb=n % 3: e.tensor_add(
                    ob[pa][:, t * 1024:(t + 1) * 1024], sgb[:, t * 1024:(t + 1) * 1024],
                    xb[pb][:, t * 1024:(t + 1) * 1024])), waits=wts, inc="G")
            final_g[n] = cnt["G"]

            row0 = (NCH - 1 - n) * C
            dst = y_d[row0:row0 + C, :].rearrange("(t p) c -> p t c", p=128)
            src = ob[pa][:].rearrange("p (t c) -> p t c", t=2)
            odma[n] = op("SY", (lambda e, d=dst, s=src: e.dma_start(d, s)),
                         waits=[W("G", final_g[n]), W("O", cnt["O"])], inc="O")

        plan["SY"].append(((W("O", cnt["O"]),), None, ()))

        def emit(eng, key, selfch=None):
            done = 0
            mark = {}
            for waits, fn, incs in plan[key]:
                if selfch is not None and incs and done > mark.get(selfch, -1):
                    eng.wait_ge(SEM[selfch], done)
                    mark[selfch] = done
                for (ch, val) in waits:
                    if val > mark.get(ch, -1):
                        eng.wait_ge(SEM[ch], val)
                        mark[ch] = val
                if fn is None:
                    continue
                ins = fn(eng)
                for (ch, step) in incs:
                    ins = ins.then_inc(SEM[ch], step)
                    if ch == selfch:
                        done += step

        with nc.Block() as block:
            @block.sync
            def _(e):
                emit(e, "SY")

            @block.tensor
            def _(e):
                emit(e, "T")

            @block.scalar
            def _(e):
                emit(e, "S", selfch="S")

            @block.vector
            def _(e):
                emit(e, "V", selfch="V")

            @block.gpsimd
            def _(e):
                emit(e, "G", selfch="G")

    return nc


def make_inmap(xb, dl, osc, qsv, ksv, G1R, G2R, G1I, G2I, brv, biv):
    cs = host_consts()
    return {
        "x": np.ascontiguousarray(xb, np.float32),
        "dl": np.full((128, 1), dl, np.float32),
        "osc": np.full((128, 1), osc, np.float32),
        "qs": np.ascontiguousarray(np.asarray(qsv, np.float32).reshape(8, 128).T),
        "ks": np.ascontiguousarray(np.asarray(ksv, np.float32).reshape(8, 128).T),
        "g1r": np.ascontiguousarray(np.asarray(G1R, np.float32).reshape(32, 128)),
        "g1i": np.ascontiguousarray(np.asarray(G1I, np.float32).reshape(32, 128)),
        "g2r": np.ascontiguousarray(np.asarray(G2R, np.float32).reshape(4, 256)),
        "g2i": np.ascontiguousarray(np.asarray(G2I, np.float32).reshape(4, 256)),
        "br": np.ascontiguousarray(np.asarray(brv, np.float32).reshape(1, 512)),
        "bi": np.ascontiguousarray(np.asarray(biv, np.float32).reshape(1, 512)),
        "selL1": cs["selL1"], "selR1": cs["selR1"], "selL2": cs["selL2"], "selR2": cs["selR2"],
        "dmaxT": cs["dmaxT"], "causT": cs["causT"], "expc": cs["expc"],
        "ident": cs["ident"], "ones1": cs["ones1"],
    }




_CACHE = {}


def kernel(L0_attn_x, L0_attn_decay_logit, L0_attn_out_scale, L0_attn_q_scale,
           L0_attn_k_scale, L0_ffn_G1_R, L0_ffn_G2_R, L0_ffn_G1_I, L0_ffn_G2_I,
           L0_ffn_bias_real, L0_ffn_bias_imag):
    x = np.ascontiguousarray(np.asarray(L0_attn_x, np.float32))
    B, T, _ = x.shape
    if T not in _CACHE:
        _CACHE[T] = build(T)
    nc = _CACHE[T]
    shared = make_inmap(x[0], np.float32(L0_attn_decay_logit), np.float32(L0_attn_out_scale),
                        L0_attn_q_scale, L0_attn_k_scale, L0_ffn_G1_R, L0_ffn_G2_R,
                        L0_ffn_G1_I, L0_ffn_G2_I, L0_ffn_bias_real, L0_ffn_bias_imag)
    in_maps = []
    for b in range(B):
        m = dict(shared)
        m["x"] = np.ascontiguousarray(x[b])
        in_maps.append(m)
    res = run_bass_kernel_spmd(nc, in_maps, core_ids=list(range(B)))
    return np.stack([np.asarray(r["y"], np.float32) for r in res.results], axis=0)
